# version 10
# baseline (speedup 1.0000x reference)
import numpy as np
import ml_dtypes
from contextlib import ExitStack

import concourse.bass as bass
import concourse.mybir as mybir
from concourse.bass_utils import run_bass_kernel_spmd

F32 = mybir.dt.float32
BF16 = mybir.dt.bfloat16
AF = mybir.ActivationFunctionType
ALU = mybir.AluOpType
AX = mybir.AxisListType.X

f32i = np.float32

NCORES = 8
D = 4096
NTOK = 16384
NROW = 2048
H = 8
E = 64
NPAIR = 4
DC = 32
SEQ = 8192
RECIP_SQRT512 = 1.0 / np.sqrt(512.0)

ENGINES = ("sync", "scalar", "vector", "tensor", "gpsimd")


class Sched:

    def __init__(self, nc):
        self.nc = nc
        self.steps = []
        self.cum = {}
        self.emitted = 0
        self.sems = {}
        self.last_step = {}
        self.water = {e: {} for e in ENGINES}
        self._bar = []
        self._stack = ExitStack()
        for e in ENGINES:
            self._get_sem(e)

    def _get_sem(self, name):
        if name not in self.sems:
            self.sems[name] = self._stack.enter_context(
                nc_sem(self.nc, name))
            self.cum[name] = 0
        return self.sems[name]

    def add(self, engine, fn, inc=1, deps=(), sem=None):
        sem = sem or engine
        self._get_sem(sem)
        sid = len(self.steps)
        dl = [d for d in deps if d is not None]
        self.steps.append(dict(engine=engine, fn=fn, inc=inc, deps=dl,
                               sem=sem, cum_after=self.cum[sem] + inc))
        self.cum[sem] += inc
        if inc > 0:
            self.last_step[sem] = sid
        return sid

    def barrier(self):
        self._bar = [s for s in self.last_step.values() if s is not None]

    def emit_pending(self):
        lo, hi = self.emitted, len(self.steps)
        self.emitted = hi
        if lo == hi:
            return
        bar = list(self._bar)
        nc = self.nc

        with nc.Block() as block:
            def make(eng_name):
                def emitter(eng):
                    for sid in range(lo, hi):
                        st = self.steps[sid]
                        if st["engine"] != eng_name:
                            continue
                        need = {}
                        for d in st["deps"] + bar:
                            ds = self.steps[d]
                            assert ds["inc"] > 0, f"dep on inc-0 step {d}"
                            v = ds["cum_after"]
                            if v > need.get(ds["sem"], 0):
                                need[ds["sem"]] = v
                        for pe, v in need.items():
                            if v > self.water[eng_name].get(pe, 0):
                                eng.wait_ge(self.sems[pe], v)
                                self.water[eng_name][pe] = v
                        inst = st["fn"](eng)
                        if st["inc"] > 0:
                            inst.then_inc(self.sems[st["sem"]], st["inc"])
                return emitter
            for e in ENGINES:
                if any(self.steps[s]["engine"] == e for s in range(lo, hi)):
                    getattr(block, e)(make(e))

    def final_join(self):
        with self.nc.Block() as block:
            @block.gpsimd
            def _(g):
                for name, v in self.cum.items():
                    if v > 0:
                        g.wait_ge(self.sems[name], v)

    def close(self):
        self._stack.close()


def nc_sem(nc, name):
    return nc.semaphore(f"s_{name}")


def build_program(debug=False, bass_debug=False):
    nc = bass.Bass("TRN2", target_bir_lowering=False, debug=bass_debug)

    def dram_in(name, shape, dt=F32):
        return nc.dram_tensor(name, list(shape), dt, kind="ExternalInput")

    xt = dram_in("xt", [D, NROW])
    wq_h = dram_in("wq_h", [NPAIR, 128, DC, 128])
    wk_h = dram_in("wk_h", [NPAIR, 128, DC, 128])
    wv_h = dram_in("wv_h", [DC, 128, 512])
    wr_h = dram_in("wr_h", [DC, 128, 8])
    wa_h = dram_in("wa_h", [DC, 128, 128])
    wcat_h = dram_in("wcat_h", [32, 128, 37, 128])
    bqT_h = dram_in("bqT_h", [128, NPAIR])
    bkT_h = dram_in("bkT_h", [128, NPAIR])
    bv_h = dram_in("bv_h", [1, 512])
    bbT_h = dram_in("bbT_h", [128, 32])
    wobT_h = dram_in("wobT_h", [128, 32])
    ident_h = dram_in("ident_h", [128, 128])
    e16_h = dram_in("e16_h", [8, 128], BF16)
    sel_h = dram_in("sel_h", [32, 2048], BF16)

    outT = nc.dram_tensor("outT", [D, NROW], F32, kind="ExternalOutput")

    cc_kin = nc.dram_tensor("cc_kin", [NPAIR, 128, NROW], BF16)
    cc_kout = nc.dram_tensor("cc_kout", [NPAIR, 512, NROW], BF16)
    cc_vin = nc.dram_tensor("cc_vin", [NPAIR, NROW, 128], BF16)
    cc_vout = nc.dram_tensor("cc_vout", [NPAIR, SEQ, 128], BF16)
    mxscr = nc.dram_tensor("mxscr", [H, 16, 128], BF16)
    qdram = nc.dram_tensor("qdram", [H, 64, NROW], BF16)
    den_dram = nc.dram_tensor("den_dram", [32, 512], F32)

    dbg = {}
    if debug:
        dbg["xn"] = nc.dram_tensor("d_xn", [128, DC, NROW], BF16, kind="ExternalOutput")
        dbg["hp"] = nc.dram_tensor("d_hp", [128, NROW], BF16, kind="ExternalOutput")
        dbg["ot"] = nc.dram_tensor("d_ot", [128, NPAIR, NROW], BF16, kind="ExternalOutput")

    sc = Sched(nc)
    A = sc.add

    with ExitStack() as top:
        tsb = lambda n, s, d: top.enter_context(nc.sbuf_tensor(n, s, d))
        ident = tsb("ident", [128, 128], F32)
        ones128 = tsb("ones128", [128, 1], BF16)
        ones1x128 = tsb("ones1x128", [1, 128], BF16)
        ones1x64 = tsb("ones1x64", [1, 64], BF16)
        ot = tsb("ot", [128, NPAIR, NROW], BF16)
        hp = tsb("hp", [128, NROW], BF16)

        s_ident = A("sync", lambda e: e.dma_start(out=ident[:, :], in_=ident_h[:, :]), 16, sem="ld1")
        LD1 = [s_ident]
        m_o1 = A("vector", lambda e: e.memset(ones128[:, :], 1.0), 1)
        m_o2 = A("vector", lambda e: e.memset(ones1x128[:, :], 1.0), 1)
        A("vector", lambda e: e.memset(ones1x64[:, :], 1.0), 1)

        xn = tsb("xn", [128, DC, NROW], BF16)

        with ExitStack() as ph:
            psb = lambda n, s, d: ph.enter_context(nc.sbuf_tensor(n, s, d))
            pps = lambda n, s, d: ph.enter_context(nc.psum_tensor(n, s, d))
            xst1 = psb("xst1", [128, 2, 4, 256], F32)
            xst2 = psb("xst2", [128, 2, 4, 256], F32)
            xbb = psb("xbb", [128, 2, 4, 256], BF16)
            sqb = psb("sqb", [128, 2, 4, 256], BF16)
            mr = psb("mr", [1, 512], F32)
            mrb = psb("mrb", [1, 512], BF16)
            ps_stat = [pps("ps_stat0", [1, 512], F32), pps("ps_stat1", [1, 512], F32)]
            ps_bc = [pps("ps_bc0", [128, 512], F32), pps("ps_bc1", [128, 512], F32)]

            NRT, NCG = 8, 8
            use1 = {}
            use2 = {}
            stat_free = {}
            bcast_free = {}
            bmm_of = {}

            def stats_part(rt):
                rsl = slice(rt * 256, (rt + 1) * 256)
                spar = rt % 2
                last_mm = None
                for cg in range(NCG):
                    par = cg % 2
                    d1 = A("sync", lambda e, cg=cg, rsl=rsl, par=par: e.dma_start(
                        out=xst1[:, par], in_=xt.ap()[cg * 512:(cg + 1) * 512, rsl]
                        .rearrange("(c p) r -> p c r", p=128)), 16,
                        deps=[use1.get(par)], sem=f"xs1{par}")
                    cst = A("gpsimd", lambda e, par=par: e.tensor_copy(
                        xbb[:, par], xst1[:, par]), 1, deps=[d1, use1.get(par)])
                    sq = A("scalar", lambda e, par=par: e.activation(
                        sqb[:, par], xst1[:, par], AF.Square), 1,
                        deps=[d1, use1.get(par)])
                    for j in range(4):
                        first = (cg == 0 and j == 0)
                        last = (cg == NCG - 1 and j == 3)
                        A("tensor", lambda e, par=par, j=j, spar=spar, first=first:
                          e.matmul(ps_stat[spar][:, 0:256], ones128[:, :],
                                   xbb[:, par, j], start=first, stop=False),
                          1, deps=[cst, m_o1] +
                          ([stat_free.get(spar)] if first else []))
                        last_mm = A(
                            "tensor", lambda e, par=par, j=j, spar=spar, last=last:
                            e.matmul(ps_stat[spar][:, 256:512], ones128[:, :],
                                     sqb[:, par, j], start=False, stop=last),
                            1, deps=[sq])
                    use1[par] = last_mm
                v1 = A("vector", lambda e, spar=spar: e.tensor_scalar(
                    mr[:, 0:256], ps_stat[spar][0:1, 0:256], 1.0 / 4096.0,
                    None, ALU.mult), 1, deps=[last_mm])
                v2 = A("vector", lambda e: e.tensor_mul(
                    mr[:, 256:512], mr[:, 0:256], mr[:, 0:256]), 1, deps=[v1])
                v3 = A("vector", lambda e, spar=spar: e.scalar_tensor_tensor(
                    mr[:, 256:512], ps_stat[spar][0:1, 256:512], 1.0 / 4096.0,
                    mr[:, 256:512], ALU.mult, ALU.subtract), 1, deps=[v2])
                s1 = A("scalar", lambda e: e.activation(
                    mr[:, 256:512], mr[:, 256:512], AF.Sqrt, bias=f32i(1e-5)),
                    1, deps=[v3])
                v4 = A("vector", lambda e: e.reciprocal(
                    mr[:, 256:512], mr[:, 256:512]), 1, deps=[s1])
                v5 = A("vector", lambda e: e.tensor_copy(mrb[:, :], mr[:, :]), 1,
                       deps=[v4])
                stat_free[spar] = v5
                bmm_of[rt] = A("tensor", lambda e, spar=spar: e.matmul(
                    ps_bc[spar][:, :], ones1x128[:, :], mrb[:, :],
                    start=True, stop=True), 1,
                    deps=[v5, m_o2, bcast_free.get(spar)])

            def apply_part(rt):
                rsl = slice(rt * 256, (rt + 1) * 256)
                spar = rt % 2
                bmm = bmm_of[rt]
                lastt = None
                for cg in range(NCG):
                    par = cg % 2
                    d2 = A("scalar", lambda e, cg=cg, rsl=rsl, par=par: e.dma_start(
                        out=xst2[:, par], in_=xt.ap()[cg * 512:(cg + 1) * 512, rsl]
                        .rearrange("(c p) r -> p c r", p=128)), 16,
                        deps=[use2.get(par)], sem=f"xs2{par}")
                    for j in range(4):
                        c = cg * 4 + j
                        t1 = A("vector", lambda e, par=par, j=j, spar=spar:
                               e.tensor_sub(xst2[:, par, j], xst2[:, par, j],
                                            ps_bc[spar][:, 0:256]),
                               1, deps=[d2, bmm])
                        lastt = A("vector", lambda e, par=par, j=j, c=c, rsl=rsl,
                                  spar=spar: e.tensor_mul(
                                      xn[:, c, rsl], xst2[:, par, j],
                                      ps_bc[spar][:, 256:512]), 1, deps=[t1])
                    use2[par] = lastt
                bcast_free[spar] = lastt

            for rt in range(NRT):
                stats_part(rt)
                if rt > 0:
                    apply_part(rt - 1)
            apply_part(NRT - 1)
            sc.emit_pending()
        sc.barrier()

        sc.barrier()

        k_coll = {}
        with ExitStack() as ph:
            psb = lambda n, s, d: ph.enter_context(nc.sbuf_tensor(n, s, d))
            pps = lambda n, s, d: ph.enter_context(nc.psum_tensor(n, s, d))
            wqs = psb("wqs", [128, 2, DC, 128], BF16)
            wks = psb("wks", [128, 2, DC, 128], BF16)
            kown = psb("kown", [128, NROW], BF16)
            qstage = psb("qstage", [64, 2, 512], BF16)
            bqT = psb("bqT", [128, NPAIR], F32)
            bkT = psb("bkT", [128, NPAIR], F32)
            LD1.append(A("sync", lambda e: e.dma_start(out=bqT[:, :],
                                                       in_=bqT_h[:, :]), 16,
                         sem="ld1"))
            LD1.append(A("sync", lambda e: e.dma_start(out=bkT[:, :],
                                                       in_=bkT_h[:, :]), 16,
                         sem="ld1"))
            ps_q = [pps("ps_q0", [128, 512], F32), pps("ps_q1", [128, 512], F32)]
            ps_k = [pps("ps_k0", [128, 512], F32), pps("ps_k1", [128, 512], F32)]

            wfree = {}
            qfree = {}
            kfree = {}
            qdma = {}
            kd = None
            for p in range(NPAIR):
                par = p % 2
                dq = A("gpsimd", lambda e, p=p, par=par: e.dma_start(
                    out=wqs[:, par], in_=wq_h.ap()[p]), 16,
                    deps=[wfree.get(("q", par))], sem=f"wq{par}")
                dk = A("gpsimd", lambda e, p=p, par=par: e.dma_start(
                    out=wks[:, par], in_=wk_h.ap()[p]), 16,
                    deps=[wfree.get(("k", par))], sem=f"wk{par}")
                qmm = kmm = None
                for rt in range(4):
                    rsl = slice(rt * 512, (rt + 1) * 512)
                    rp = rt % 2
                    for c in range(DC):
                        qmm = A("tensor", lambda e, par=par, c=c, rsl=rsl, rp=rp:
                                e.matmul(ps_q[rp][:, :], wqs[:, par, c],
                                         xn[:, c, rsl], start=(c == 0),
                                         stop=(c == DC - 1)), 1,
                                deps=([dq, qfree.get(rp)] if c == 0 else []))
                    for c in range(DC):
                        kmm = A("tensor", lambda e, par=par, c=c, rsl=rsl, rp=rp:
                                e.matmul(ps_k[rp][:, :], wks[:, par, c],
                                         xn[:, c, rsl], start=(c == 0),
                                         stop=(c == DC - 1)), 1,
                                deps=([dk, kfree.get(rp)] if c == 0 else []))
                    eva = A("vector", lambda e, p=p, rp=rp: e.tensor_scalar(
                        qstage[:, 0, :], ps_q[rp][0:64, :],
                        bqT[0:64, p:p + 1], None, ALU.add), 1,
                        deps=[qmm] + LD1 + [qdma.get(0)])
                    evb = A("vector", lambda e, p=p, rp=rp: e.tensor_scalar(
                        qstage[:, 1, :], ps_q[rp][64:128, :],
                        bqT[64:128, p:p + 1], None, ALU.add), 1,
                        deps=[qmm, qdma.get(1)])
                    qdma[0] = A("sync", lambda e, p=p, rsl=rsl: e.dma_start(
                        out=qdram.ap()[2 * p][:, rsl], in_=qstage[:, 0, :]), 16,
                        deps=[eva], sem="qo0")
                    qdma[1] = A("sync", lambda e, p=p, rsl=rsl: e.dma_start(
                        out=qdram.ap()[2 * p + 1][:, rsl], in_=qstage[:, 1, :]), 16,
                        deps=[evb], sem="qo1")
                    ev2 = evb
                    ev3 = A("vector", lambda e, p=p, rsl=rsl, rp=rp: e.tensor_scalar(
                        kown[:, rsl], ps_k[rp][:, :],
                        bkT[:, p:p + 1], None, ALU.add), 1, deps=[kmm] + LD1 + [kd])
                    qfree[rp] = ev2
                    kfree[rp] = ev3
                wfree[("q", par)] = qmm
                wfree[("k", par)] = kmm
                kd = A("sync", lambda e, p=p: e.dma_start(
                    out=cc_kin.ap()[p], in_=kown[:, :]), 16,
                    deps=[kfree[0], kfree[1]], sem="kout")
                k_coll[p] = A("gpsimd", lambda e, p=p: e.collective_compute(
                    "AllGather", ALU.bypass,
                    replica_groups=[[0, 1, 2, 3], [4, 5, 6, 7]],
                    ins=[cc_kin.ap()[p]], outs=[cc_kout.ap()[p]]), 1,
                    deps=[kd], sem=f"kc{p}")
            sc.emit_pending()
        sc.barrier()

        v_coll = {}
        with ExitStack() as ph:
            psb = lambda n, s, d: ph.enter_context(nc.sbuf_tensor(n, s, d))
            pps = lambda n, s, d: ph.enter_context(nc.psum_tensor(n, s, d))
            wvs = psb("wvs", [128, DC, 512], BF16)
            wrs = psb("wrs", [128, DC, 8], BF16)
            was = psb("was", [128, DC, 128], BF16)
            e16 = psb("e16", [8, 128], BF16)
            bvb = psb("bvb", [128, 512], BF16)
            LD1.append(A("sync", lambda e: e.dma_start(out=e16[:, :],
                                                       in_=e16_h[:, :]), 16,
                         sem="ld1"))
            vev = psb("vev", [128, 2, 512], BF16)
            rsc = psb("rsc", [128, 8], F32)
            rs2 = psb("rs2", [128, 8], F32)
            rcol = psb("rcol", [128, 1], F32)
            rcol2 = psb("rcol2", [128, 1], F32)
            combf = psb("combf", [128, 8], F32)
            combT_sb = psb("combT_sb", [8, 512], BF16)
            exq = psb("exq", [128, 4, 512], BF16)
            bvrow = psb("bvrow", [1, 512], BF16)
            ps_v = pps("ps_v", [128, 512], F32)
            ps_lg = pps("ps_lg", [128, 8], F32)
            ps_ct = pps("ps_ct", [8, 512], F32)
            ps_ex = pps("ps_ex", [128, 512], F32)
            ps_hh = pps("ps_hh", [128, 512], F32)

            B2W = []
            for i in range(4):
                B2W.append(A("gpsimd", lambda e, i=i: e.dma_start(
                    out=wvs[:, i * 8:(i + 1) * 8, :],
                    in_=wv_h.ap()[i * 8:(i + 1) * 8].rearrange("c p m -> p c m")),
                    16, sem="b2w"))
            B2W.append(A("gpsimd", lambda e: e.dma_start(
                out=wrs[:, :, :], in_=wr_h.ap().rearrange("c p m -> p c m")), 16,
                sem="b2w"))
            for i in range(4):
                B2W.append(A("gpsimd", lambda e, i=i: e.dma_start(
                    out=was[:, i * 8:(i + 1) * 8, :],
                    in_=wa_h.ap()[i * 8:(i + 1) * 8].rearrange("c p m -> p c m")),
                    16, sem="b2w"))
            B2W.append(A("gpsimd", lambda e: e.dma_start(
                out=bvrow[:, :], in_=bv_h[:, :]), 16, sem="b2w"))
            bvmm = A("tensor", lambda e: e.matmul(
                ps_ex[:, :], ones1x128[:, :], bvrow[:, :], start=True, stop=True),
                1, deps=B2W)
            bvcp = A("vector", lambda e: e.tensor_copy(bvb[:, :], ps_ex[:, :]),
                     1, deps=[bvmm])

            ev_prev = None
            ct_free = None
            exq_ids = []
            vd = None
            lg_free = []
            for rt5 in range(4):
                ct_ids = []
                for sub in range(4):
                    rt = rt5 * 4 + sub
                    rsl = slice(rt * 128, (rt + 1) * 128)
                    vmm = lgmm = None
                    for c in range(DC):
                        vmm = A("tensor", lambda e, c=c, rsl=rsl: e.matmul(
                            ps_v[:, :], xn[:, c, rsl], wvs[:, c, :],
                            start=(c == 0), stop=(c == DC - 1)), 1,
                            deps=(B2W + [ev_prev] if c == 0 else []))
                    for c in range(DC):
                        lgmm = A("tensor", lambda e, c=c, rsl=rsl: e.matmul(
                            ps_lg[:, :], xn[:, c, rsl], wrs[:, c, :],
                            start=(c == 0), stop=(c == DC - 1)), 1,
                            deps=((B2W + lg_free) if c == 0 else []))
                    ev = A("vector", lambda e, sub=sub: e.tensor_add(
                        vev[:, sub % 2, :], ps_v[:, :], bvb[:, :]), 1,
                        deps=[vmm, bvcp, vd])
                    ev_prev = ev
                    vd = A("sync", lambda e, rt=rt, sub=sub: e.dma_start(
                        out=cc_vin.ap().rearrange("q (t p) e -> t p q e", p=128)[rt],
                        in_=vev[:, sub % 2, :]
                        .rearrange("p (q e) -> p q e", q=4)), 16, deps=[ev],
                        sem="vout")
                    r1 = A("vector", lambda e: e.reduce_max(rcol[:, :], ps_lg[:, :],
                                                            axis=AX), 1, deps=[lgmm])
                    r2 = A("vector", lambda e: e.tensor_single_scalar(
                        rsc[:, :], ps_lg[:, :], rcol[:, 0:1], ALU.is_equal),
                        1, deps=[r1])
                    r3 = A("vector", lambda e: e.scalar_tensor_tensor(
                        rs2[:, :], rsc[:, :], -1e30, ps_lg[:, :],
                        ALU.mult, ALU.add), 1, deps=[r2])
                    r4 = A("vector", lambda e: e.reduce_max(rcol2[:, :], rs2[:, :],
                                                            axis=AX), 1, deps=[r3])
                    r5 = A("vector", lambda e: e.tensor_single_scalar(
                        rsc[:, :], ps_lg[:, :], rcol2[:, 0:1], ALU.is_ge),
                        1, deps=[r4])
                    r6 = A("vector", lambda e: e.tensor_scalar(
                        rcol[:, :], rcol[:, :], -1.0, None, ALU.mult),
                        1, deps=[r2])
                    s2 = A("scalar", lambda e: e.activation(
                        rs2[:, :], ps_lg[:, :], AF.Exp, bias=rcol[:, 0:1]),
                        1, deps=[r6, r4])
                    s3 = A("scalar", lambda e: e.activation(
                        rcol2[:, :], rcol2[:, :], AF.Exp, bias=rcol[:, 0:1]),
                        1, deps=[s2, r5])
                    r7 = A("vector", lambda e: e.tensor_scalar(
                        rcol2[:, :], rcol2[:, :], 1.0, None, ALU.add),
                        1, deps=[s3])
                    r8 = A("vector", lambda e: e.reciprocal(rcol2[:, :],
                                                            rcol2[:, :]),
                           1, deps=[r7])
                    r9 = A("vector", lambda e: e.tensor_mul(
                        rs2[:, :], rs2[:, :], rsc[:, :]), 1, deps=[s2, r5])
                    r10 = A("vector", lambda e: e.tensor_scalar(
                        combf[:, :], rs2[:, :], rcol2[:, 0:1], None, ALU.mult),
                        1, deps=[r9, r8])
                    lg_free = [s2, r5]
                    tmm = A("tensor", lambda e, sub=sub: e.transpose(
                        ps_ct[:, sub * 128:(sub + 1) * 128], combf[:, :],
                        ident[:, :]), 1, deps=[r10, ct_free] + LD1)
                    ct_ids.append(tmm)
                cte = A("vector", lambda e: e.tensor_copy(combT_sb[:, :],
                                                          ps_ct[:, :]),
                        1, deps=ct_ids)
                ct_free = cte
                exm = A("tensor", lambda e: e.matmul(
                    ps_ex[:, :], e16[:, :], combT_sb[:, :], start=True, stop=True),
                    1, deps=[cte, bvcp] + LD1)
                exq_ids.append(A("vector", lambda e, rt5=rt5: e.tensor_copy(
                    exq[:, rt5, :], ps_ex[:, :]), 1, deps=[exm]))
            hh_free = None
            hp_done = None
            for rt5 in range(4):
                hmm = None
                for c in range(DC):
                    hmm = A("tensor", lambda e, c=c, rt5=rt5: e.matmul(
                        ps_hh[:, :], was[:, c, :],
                        xn[:, c, rt5 * 512:(rt5 + 1) * 512],
                        start=(c == 0), stop=(c == DC - 1)), 1,
                        deps=(B2W + [exq_ids[rt5], hh_free]
                              if c == 0 else []))
                hp_done = A("vector", lambda e, rt5=rt5: e.tensor_mul(
                    hp[:, rt5 * 512:(rt5 + 1) * 512], ps_hh[:, :],
                    exq[:, rt5, :]), 1, deps=[hmm])
                hh_free = hp_done
            for p in range(NPAIR):
                v_coll[p] = A("gpsimd", lambda e, p=p: e.collective_compute(
                    "AllGather", ALU.bypass,
                    replica_groups=[[0, 1, 2, 3], [4, 5, 6, 7]],
                    ins=[cc_vin.ap()[p]], outs=[cc_vout.ap()[p]]), 1,
                    deps=[vd], sem=f"vc{p}")
            if debug:
                A("sync", lambda e: e.dma_start(out=dbg["xn"].ap(), in_=xn.ap()),
                  16, deps=[hp_done], sem="dbgd")
                A("sync", lambda e: e.dma_start(out=dbg["hp"].ap(), in_=hp.ap()),
                  16, deps=[hp_done], sem="dbgd")
            sc.emit_pending()
        sc.barrier()

        with ExitStack() as ph:
            psb = lambda n, s, d: ph.enter_context(nc.sbuf_tensor(n, s, d))
            pps = lambda n, s, d: ph.enter_context(nc.psum_tensor(n, s, d))
            ktx = psb("ktx", [65, 2, SEQ], BF16)
            qtxh = psb("qtxh", [65, 2, NROW], BF16)
            vst = psb("vst", [128, 64, 65], BF16)
            pt = psb("pt", [128, 1536], BF16)
            MX16 = psb("MX16", [128, 11], F32)
            MX = psb("MX", [128, 16], F32)
            MXT = psb("MXT", [16, 128], BF16)
            denst = psb("denst", [1, 512], F32)
            ps_nat = [pps("ps_natA", [128, 1024], F32),
                      pps("ps_natB", [128, 512], F32)]
            ps_s = [pps("ps_sA", [128, 1024], F32),
                    pps("ps_sB", [128, 512], F32)]
            ps_o = [pps("ps_oA", [65, 512], F32), pps("ps_oB", [65, 512], F32)]

            mset0 = A("vector", lambda e: e.memset(ktx[64:65, 0, :], 1.0), 1)
            mset1 = A("vector", lambda e: e.memset(ktx[64:65, 1, :], 1.0), 1)
            msetv = A("vector", lambda e: e.memset(vst[:, :, 64:65], 1.0), 1)

            st = {
                "ktx_free": {0: mset0, 1: mset1},
                "q_free": {0: None, 1: None},
                "vst_free": msetv,
                "pt_free": {0: None, 1: None},
                "nat_free": {0: None, 1: None},
                "o_free": {0: None, 1: None},
                "mx_free": None,
                "mx16_free": None,
                "exp_last": None,
                "s_free": {0: None, 1: None},
                "sbuf_rr": 0,
                "dend": None,
                "kld": {}, "vld": {}, "qtxrow": {},
            }

            def load_ktx(h):
                p, eo, b = h // 2, (h % 2) * 64, h % 2
                kdm = A("sync", lambda e, p=p, eo=eo, b=b: e.dma_start(
                    out=ktx[0:64, b, :].rearrange("p (r f) -> p r f", r=4),
                    in_=cc_kout.ap()[p].rearrange("(r p) f -> r p f", p=128)
                    [:, eo:eo + 64, :].rearrange("r p f -> p r f")), 16,
                    deps=[k_coll[p], st["ktx_free"][b]], sem=f"kt{b}")
                qd = A("sync", lambda e, h=h, b=b: e.dma_start(
                    out=qtxh[0:64, b, :], in_=qdram.ap()[h]), 16,
                    deps=[st["q_free"][b], qdma[0], qdma[1]], sem=f"qt{b}")
                st["kld"][h] = [kdm, qd]

            def load_vst(h):
                p, eo = h // 2, (h % 2) * 64
                last = None
                for q in range(4):
                    last = A("sync", lambda e, p=p, eo=eo, q=q: e.dma_start(
                        out=vst[:, q * 16:(q + 1) * 16, 0:64],
                        in_=cc_vout.ap()[p][q * 2048:(q + 1) * 2048, eo:eo + 64]
                        .rearrange("(c p) e -> p c e", p=128)), 16,
                        deps=[v_coll[p], st["vst_free"]], sem="vs0")
                st["vld"][h] = last

            def nat_grp(h, qt, gi):
                b = h % 2
                buf = gi % 2
                if buf == 0:
                    nch = 2 if gi < 10 else 1
                    base = (gi // 2) * 3
                else:
                    nch = 1
                    base = (gi // 2) * 3 + 2
                mm = None
                for j in range(nch):
                    scn = base + j
                    mm = A("tensor", lambda e, h=h, qt=qt, scn=scn,
                           b=b, buf=buf, j=j: e.matmul(
                               ps_nat[buf][:, j * 512:(j + 1) * 512],
                               qtxh[0:64, b, qt * 128:(qt + 1) * 128],
                               ktx[0:64, b, scn * 512:(scn + 1) * 512],
                               start=True, stop=True),
                           (1 if j == nch - 1 else 0),
                           deps=st["kld"][h] +
                           ([st["nat_free"][buf]] if j == 0 else []))
                r = A("vector", lambda e, gi=gi, buf=buf, nch=nch: e.reduce_max(
                    MX16[:, gi:gi + 1], ps_nat[buf][:, 0:nch * 512], axis=AX),
                    1, deps=[mm] + ([st["mx16_free"]] if gi == 0 else []))
                st["nat_free"][buf] = r
                return r

            def nat_fin(h, qt, lastr):
                fin = A("vector", lambda e, qt=qt: e.reduce_max(
                    MX[:, qt:qt + 1], MX16[:, 0:11], axis=AX), 1,
                    deps=[lastr, st["mx_free"]])
                st["mx16_free"] = fin
                return fin

            def mx_finalize(h, lastred):
                tr = A("tensor", lambda e: e.transpose(
                    ps_nat[1][0:16, 0:128], MX[:, :], ident[:, :]), 1,
                    deps=[lastred, st["nat_free"][1]] + LD1)
                ng = A("scalar", lambda e: e.mul(MXT[:, :],
                                                 ps_nat[1][0:16, 0:128],
                                                 f32i(-1.0)), 1, deps=[tr])
                st["nat_free"][1] = ng
                st["mx_free"] = ng
                d1 = A("gpsimd", lambda e, h=h: e.dma_start(
                    out=mxscr.ap()[h], in_=MXT[:, :]), 16, deps=[ng], sem="mxs")
                st["qtxrow"][h] = A("gpsimd", lambda e, h=h, b=h % 2: e.dma_start(
                    out=qtxh[64:65, b, :],
                    in_=mxscr.ap()[h].rearrange("a b -> (a b)")
                    .rearrange("(o f) -> o f", o=1)), 16, deps=[d1],
                    sem=f"qr{h % 2}")

            PTOFF = {0: 0, 1: 1024}
            SBW = {0: 2, 1: 1}

            def emit_attnv(ts, sub, grp):
                h = ts["h"]
                b = h % 2
                buf, nch, coff, ex = grp
                ob = sub["ob"]
                poff = PTOFF[buf]
                for j in range(nch):
                    scn = coff + j
                    first = (coff == 0 and j == 0)
                    last = (coff + j == 63)
                    av = A("tensor", lambda e, b=b, scn=scn, j=j, poff=poff,
                           ob=ob, first=first, last=last: e.matmul(
                               ps_o[ob][:, :], vst[:, scn, 0:65],
                               pt[:, poff + j * 512:poff + (j + 1) * 512],
                               start=first, stop=last),
                           (1 if j == nch - 1 else 0),
                           deps=([ex, st["vld"][h]] +
                                 ([st["o_free"][ob]] if first else [])
                                 if j == 0 else []))
                    st["pt_free"][buf] = av
                    sub["av_last"] = av

            def drain_buf(ts, buf):
                for su in ts["subs"]:
                    if su["grpq"] and su["grpq"][0][0] == buf:
                        emit_attnv(ts, su, su["grpq"].pop(0))

            def sub_step(ts, sub):
                h = ts["h"]
                b = h % 2
                qt = sub["qt"]
                qsl = slice(qt * 512, (qt + 1) * 512)
                if sub["coff"] < 64:
                    buf = st["sbuf_rr"] % 2
                    st["sbuf_rr"] += 1
                    drain_buf(ts, buf)
                    nch = min(SBW[buf], 64 - sub["coff"])
                    coff = sub["coff"]
                    sub["coff"] += nch
                    poff = PTOFF[buf]
                    smm = None
                    for j in range(nch):
                        scn = coff + j
                        smm = A("tensor", lambda e, h=h, b=b, scn=scn, j=j,
                                buf=buf, qsl=qsl: e.matmul(
                                    ps_s[buf][:, j * 512:(j + 1) * 512],
                                    ktx[:, b, scn * 128:(scn + 1) * 128],
                                    qtxh[:, b, qsl], start=True, stop=True),
                                (1 if j == nch - 1 else 0),
                                deps=([st["s_free"][buf], st["qtxrow"][h]]
                                      + st["kld"][h] if j == 0 else []))
                    ex = A("scalar", lambda e, buf=buf, poff=poff, nch=nch:
                           e.activation(pt[:, poff:poff + nch * 512],
                                        ps_s[buf][:, 0:nch * 512], AF.Exp), 1,
                           deps=[smm, st["pt_free"][buf]])
                    st["exp_last"] = ex
                    st["s_free"][buf] = ex
                    sub["grpq"].append((buf, nch, coff, ex))
                elif sub["grpq"]:
                    emit_attnv(ts, sub, sub["grpq"].pop(0))
                if sub["coff"] >= 64 and not sub["grpq"]:
                    flush_sub(ts, sub)
                    if ts["next_qt"] < 4:
                        sub["qt"] = ts["next_qt"]
                        ts["next_qt"] += 1
                        sub["coff"] = 0
                    else:
                        sub["done"] = True

            def flush_sub(ts, sub):
                h = ts["h"]
                p, eo = h // 2, (h % 2) * 64
                qt = sub["qt"]
                ob = sub["ob"]
                qsl = slice(qt * 512, (qt + 1) * 512)
                dc = A("vector", lambda e, ob=ob: e.tensor_copy(
                    denst[:, :], ps_o[ob][64:65, :]), 1,
                    deps=[sub["av_last"], st["dend"]])
                st["dend"] = A("gpsimd", lambda e, h=h, qt=qt: e.dma_start(
                    out=den_dram.ap()[h * 4 + qt:h * 4 + qt + 1, :],
                    in_=denst[:, :]), 16, deps=[dc], sem="dnd")
                o1 = A("vector", lambda e, eo=eo, p=p, qsl=qsl, ob=ob:
                       e.tensor_copy(ot[eo:eo + 64, p, qsl],
                                     ps_o[ob][0:64, :]), 1, deps=[dc])
                st["o_free"][ob] = o1

            def tr_step(ts):
                subs = [su for su in ts["subs"] if not su["done"]]
                if not subs:
                    return False
                su = subs[ts["rr"] % len(subs)]
                ts["rr"] += 1
                sub_step(ts, su)
                return True

            def tr_advance(ts, n):
                for _ in range(n):
                    if not tr_step(ts):
                        return

            load_ktx(0)
            load_vst(0)
            st["exp_of"] = {0: None, 1: None}
            for slot in range(H + 1):
                h_nat = slot if slot < H else None
                h_tr = slot - 1 if slot >= 1 else None
                ts = ({"h": h_tr, "rr": 0, "next_qt": 2,
                       "subs": [{"qt": 0, "ob": 0, "coff": 0, "grpq": [],
                                 "av_last": None, "done": False},
                                {"qt": 1, "ob": 1, "coff": 0, "grpq": [],
                                 "av_last": None, "done": False}]}
                      if h_tr is not None else None)
                if h_nat is not None:
                    for qt in range(16):
                        lastr = None
                        for gi in range(11):
                            lastr = nat_grp(h_nat, qt, gi)
                            if ts is not None:
                                tr_advance(ts, 1)
                        lastred = nat_fin(h_nat, qt, lastr)
                    if ts is not None:
                        while tr_step(ts):
                            pass
                    mx_finalize(h_nat, lastred)
                else:
                    while tr_step(ts):
                        pass
                if h_tr is not None:
                    st["ktx_free"][h_tr % 2] = st["exp_last"]
                    st["q_free"][h_tr % 2] = st["exp_last"]
                    st["vst_free"] = max(su["av_last"] for su in ts["subs"])
                    if h_tr + 1 < H:
                        load_vst(h_tr + 1)
                if h_nat is not None and h_nat + 1 < H:
                    load_ktx(h_nat + 1)
            if debug:
                A("sync", lambda e: e.dma_start(out=dbg["ot"].ap(), in_=ot.ap()),
                  16, deps=[st["o_free"]], sem="dbgd")
            sc.emit_pending()
        sc.barrier()

        with ExitStack() as ph:
            psb = lambda n, s, d: ph.enter_context(nc.sbuf_tensor(n, s, d))
            pps = lambda n, s, d: ph.enter_context(nc.psum_tensor(n, s, d))
            denall = psb("denall", [32, 512], F32)
            recipall = psb("recipall", [32, 512], BF16)
            selc = psb("selc", [32, 2048], BF16)
            bcn = psb("bcn", [128, 2, 512], BF16)
            ps_bcn = [pps("ps_bcn0", [64, 512], F32), pps("ps_bcn1", [64, 512], F32)]

            dsel = A("sync", lambda e: e.dma_start(out=selc[:, :], in_=sel_h[:, :]),
                     16, sem="ld1")
            dden = A("sync", lambda e: e.dma_start(out=denall[:, :],
                                                   in_=den_dram.ap()), 16,
                     sem="ld1")
            nl = A("scalar", lambda e: e.activation(
                denall[:, :], denall[:, :], AF.Ln), 1, deps=[dden])
            ne = A("scalar", lambda e: e.activation(
                recipall[:, :], denall[:, :], AF.Exp, scale=f32i(-1.0),
                bias=f32i(np.log(RECIP_SQRT512))), 1, deps=[nl])
            bmf = {}
            bef = {}
            otf = {}
            for hq in range(32):
                h, qt = hq // 4, hq % 4
                eo, p = (h % 2) * 64, h // 2
                par = hq % 2
                qsl = slice(qt * 512, (qt + 1) * 512)
                bm = A("tensor", lambda e, hq=hq, par=par: e.matmul(
                    ps_bcn[par][:, :], selc[:, hq * 64:(hq + 1) * 64],
                    recipall[:, :], start=True, stop=True), 1,
                    deps=[ne, dsel, dden, bef.get(par)])
                be = A("vector", lambda e, par=par, eo=eo: e.tensor_copy(
                    bcn[eo:eo + 64, par, :], ps_bcn[par][:, :]), 1,
                    deps=[bm, otf.get(par)])
                bef[par] = be
                otf[par] = A("vector", lambda e, eo=eo, p=p, qsl=qsl, par=par:
                             e.tensor_mul(ot[eo:eo + 64, p, qsl],
                                          ot[eo:eo + 64, p, qsl],
                                          bcn[eo:eo + 64, par, :]), 1,
                             deps=[be])
            sc.emit_pending()
        sc.barrier()

        with ExitStack() as ph:
            psb = lambda n, s, d: ph.enter_context(nc.sbuf_tensor(n, s, d))
            pps = lambda n, s, d: ph.enter_context(nc.psum_tensor(n, s, d))
            wcs = psb("wcs", [128, 2, 37, 128], BF16)
            xres = psb("xres", [128, 2, 1024], F32)
            ostg = psb("ostg", [128, 4, 512], F32)
            cbias = psb("cbias", [128, 32], F32)
            cb2 = psb("cb2", [128, 32], F32)
            ps_f = [pps("ps_f0", [128, 512], F32), pps("ps_f1", [128, 512], F32)]

            xnl = []
            b1 = A("sync", lambda e: e.dma_start(out=cbias[:, :], in_=bbT_h[:, :]),
                   16, sem="xnr")
            b2 = A("sync", lambda e: e.dma_start(out=cb2[:, :], in_=wobT_h[:, :]),
                   16, sem="xnr")
            b3 = A("vector", lambda e: e.tensor_add(cbias[:, :], cbias[:, :],
                                                    cb2[:, :]), 1, deps=[b1, b2])
            wfree = {}
            xfree = {0: None, 1: None}
            evfree = {0: None, 1: None}
            od = None
            for blk in range(32):
                par = blk % 2
                wd = A("gpsimd", lambda e, blk=blk, par=par: e.dma_start(
                    out=wcs[:, par], in_=wcat_h.ap()[blk]), 16,
                    deps=[wfree.get(par)], sem=f"wc{par}")
                xd = {}
                for hf in range(2):
                    xd[hf] = A("scalar", lambda e, blk=blk, hf=hf: e.dma_start(
                        out=xres[:, hf],
                        in_=xt.ap()[blk * 128:(blk + 1) * 128,
                                    hf * 1024:(hf + 1) * 1024]), 16,
                        deps=[xfree[hf]], sem=f"xr{hf}")
                mm = None
                for rt in range(4):
                    rp = rt % 2
                    rsl = slice(rt * 512, (rt + 1) * 512)
                    for c in range(DC):
                        mm = A("tensor", lambda e, par=par, c=c, rsl=rsl, rp=rp:
                               e.matmul(ps_f[rp][:, :], wcs[:, par, c],
                                        xn[:, c, rsl], start=(c == 0),
                                        stop=False), 1,
                               deps=([wd, evfree[rp], b1, b2] + xnl
                                     if c == 0 else []))
                    mm = A("tensor", lambda e, par=par, rsl=rsl, rp=rp:
                           e.matmul(ps_f[rp][:, :], wcs[:, par, 32],
                                    hp[:, rsl], start=False, stop=False), 1)
                    for j in range(4):
                        mm = A("tensor", lambda e, par=par, j=j, rsl=rsl, rp=rp:
                               e.matmul(ps_f[rp][:, :], wcs[:, par, 33 + j],
                                        ot[:, j, rsl], start=False,
                                        stop=(j == 3)), 1)
                    ev = A("vector", lambda e, blk=blk, rt=rt, rp=rp:
                           e.scalar_tensor_tensor(
                               ostg[:, rt, :], ps_f[rp][:, :],
                               cbias[:, blk:blk + 1],
                               xres[:, rt // 2, (rt % 2) * 512:
                                    (rt % 2) * 512 + 512],
                               ALU.add, ALU.add), 1,
                           deps=[mm, b3, xd[rt // 2], od])
                    evfree[rp] = ev
                    if rt == 1:
                        xfree[0] = ev
                    if rt == 3:
                        xfree[1] = ev
                        od = A("sync", lambda e, blk=blk: e.dma_start(
                            out=outT.ap()[blk * 128:(blk + 1) * 128, :],
                            in_=ostg.ap().rearrange("p a b -> p (a b)")), 16,
                            deps=[ev], sem="oout")
                wfree[par] = mm
            sc.emit_pending()

        sc.final_join()
    sc.close()
    return nc



def _prep_weights(ins):
    w = {}
    Wqk = np.asarray(ins["L0_attn_Wqk_w"], dtype=np.float32)
    bqk = np.asarray(ins["L0_attn_Wqk_b"], dtype=np.float32)
    Wv = np.asarray(ins["L0_attn_Wv_w"], dtype=np.float32)
    bv = np.asarray(ins["L0_attn_Wv_b"], dtype=np.float32)
    Wo = np.asarray(ins["L0_attn_Wo_w"], dtype=np.float32)
    bo = np.asarray(ins["L0_attn_Wo_b"], dtype=np.float32)
    Wb = np.asarray(ins["L0_ffn_W_base"], dtype=np.float32)
    bb = np.asarray(ins["L0_ffn_b_base"], dtype=np.float32)
    Wr = np.asarray(ins["L0_ffn_W_router"], dtype=np.float32)
    A_ = np.asarray(ins["L0_ffn_A_stack"], dtype=np.float32)
    B_ = np.asarray(ins["L0_ffn_B_stack"], dtype=np.float32)

    Wq = Wqk.reshape(H, E, 2, D)[:, :, 0, :].reshape(512, D)
    Wk = Wqk.reshape(H, E, 2, D)[:, :, 1, :].reshape(512, D)
    bq = bqk.reshape(H, E, 2)[:, :, 0].reshape(512)
    bk = bqk.reshape(H, E, 2)[:, :, 1].reshape(512)

    w["wq_h"] = np.ascontiguousarray(
        Wq.reshape(NPAIR, 128, DC, 128).transpose(0, 3, 2, 1))
    w["wk_h"] = np.ascontiguousarray(
        Wk.reshape(NPAIR, 128, DC, 128).transpose(0, 3, 2, 1))
    w["wv_h"] = np.ascontiguousarray(Wv.T.reshape(DC, 128, 512))
    w["wr_h"] = np.ascontiguousarray(Wr.T.reshape(DC, 128, 8))
    A_all = A_.reshape(128, D)
    w["wa_h"] = np.ascontiguousarray(A_all.T.reshape(DC, 128, 128))
    B_allT = B_.transpose(0, 2, 1).reshape(128, D)
    Kcat = np.concatenate([Wb.T, B_allT, Wo.T], axis=0)
    w["wcat_h"] = np.ascontiguousarray(
        Kcat.reshape(37, 128, 32, 128).transpose(2, 1, 0, 3))
    w["bqT_h"] = np.ascontiguousarray(bq.reshape(NPAIR, 128).T)
    w["bkT_h"] = np.ascontiguousarray(bk.reshape(NPAIR, 128).T)
    w["bv_h"] = np.ascontiguousarray(bv.reshape(1, 512))
    w["bbT_h"] = np.ascontiguousarray(bb.reshape(32, 128).T)
    w["wobT_h"] = np.ascontiguousarray(bo.reshape(32, 128).T)
    w["ident_h"] = np.eye(128, dtype=np.float32)
    sel = np.zeros((32, 32 * 64), dtype=np.float32)
    for k in range(32):
        sel[k, k * 64:(k + 1) * 64] = 1.0
    w["sel_h"] = sel.astype(ml_dtypes.bfloat16)
    e16 = np.zeros((8, 128), dtype=np.float32)
    for ex in range(8):
        e16[ex, ex * 16:(ex + 1) * 16] = 2.0
    w["e16_h"] = e16.astype(ml_dtypes.bfloat16)
    return w


def kernel(**inputs):
    x = np.asarray(inputs["L0_attn_x"], dtype=np.float32).reshape(NTOK, D)
    w = _prep_weights(inputs)
    nc = build_program(debug=False)
    in_maps = []
    for c in range(NCORES):
        m = dict(w)
        m["xt"] = np.ascontiguousarray(x[c * NROW:(c + 1) * NROW, :].T)
        in_maps.append(m)
    res = run_bass_kernel_spmd(nc, in_maps, core_ids=list(range(NCORES)))
    out = np.empty((NTOK, D), dtype=np.float32)
    for c in range(NCORES):
        out[c * NROW:(c + 1) * NROW, :] = res.results[c]["outT"].T
    return out.reshape(2, 8192, D)


# version 11 (final)
# speedup vs baseline: 1.4853x; 1.4853x over previous
import numpy as np
import ml_dtypes
from contextlib import ExitStack

import concourse.bass as bass
import concourse.mybir as mybir
from concourse.bass_utils import run_bass_kernel_spmd

F32 = mybir.dt.float32
BF16 = mybir.dt.bfloat16
AF = mybir.ActivationFunctionType
ALU = mybir.AluOpType
AX = mybir.AxisListType.X

f32i = np.float32

NCORES = 8
D = 4096
NTOK = 16384
NROW = 2048
H = 8
E = 64
NPAIR = 4
DC = 32
SEQ = 8192
RECIP_SQRT512 = 1.0 / np.sqrt(512.0)

ENGINES = ("sync", "scalar", "vector", "tensor", "gpsimd")


class Sched:

    def __init__(self, nc):
        self.nc = nc
        self.steps = []
        self.cum = {}
        self.emitted = 0
        self.sems = {}
        self.last_step = {}
        self.water = {e: {} for e in ENGINES}
        self._bar = []
        self._stack = ExitStack()
        for e in ENGINES:
            self._get_sem(e)

    def _get_sem(self, name):
        if name not in self.sems:
            self.sems[name] = self._stack.enter_context(
                nc_sem(self.nc, name))
            self.cum[name] = 0
        return self.sems[name]

    def add(self, engine, fn, inc=1, deps=(), sem=None):
        sem = sem or engine
        self._get_sem(sem)
        sid = len(self.steps)
        dl = [d for d in deps if d is not None]
        self.steps.append(dict(engine=engine, fn=fn, inc=inc, deps=dl,
                               sem=sem, cum_after=self.cum[sem] + inc))
        self.cum[sem] += inc
        if inc > 0:
            self.last_step[sem] = sid
        return sid

    def barrier(self):
        self._bar = [s for s in self.last_step.values() if s is not None]

    def emit_pending(self):
        lo, hi = self.emitted, len(self.steps)
        self.emitted = hi
        if lo == hi:
            return
        bar = list(self._bar)
        nc = self.nc

        with nc.Block() as block:
            def make(eng_name):
                def emitter(eng):
                    for sid in range(lo, hi):
                        st = self.steps[sid]
                        if st["engine"] != eng_name:
                            continue
                        need = {}
                        for d in st["deps"] + bar:
                            ds = self.steps[d]
                            assert ds["inc"] > 0, f"dep on inc-0 step {d}"
                            v = ds["cum_after"]
                            if v > need.get(ds["sem"], 0):
                                need[ds["sem"]] = v
                        for pe, v in need.items():
                            if v > self.water[eng_name].get(pe, 0):
                                eng.wait_ge(self.sems[pe], v)
                                self.water[eng_name][pe] = v
                        inst = st["fn"](eng)
                        if st["inc"] > 0:
                            inst.then_inc(self.sems[st["sem"]], st["inc"])
                return emitter
            for e in ENGINES:
                if any(self.steps[s]["engine"] == e for s in range(lo, hi)):
                    getattr(block, e)(make(e))

    def final_join(self):
        with self.nc.Block() as block:
            @block.gpsimd
            def _(g):
                for name, v in self.cum.items():
                    if v > 0:
                        g.wait_ge(self.sems[name], v)

    def close(self):
        self._stack.close()


def nc_sem(nc, name):
    return nc.semaphore(f"s_{name}")


def build_program(debug=False, bass_debug=False):
    nc = bass.Bass("TRN2", target_bir_lowering=False, debug=bass_debug)

    def dram_in(name, shape, dt=F32):
        return nc.dram_tensor(name, list(shape), dt, kind="ExternalInput")

    xt = dram_in("xt", [D, NROW])
    wq_h = dram_in("wq_h", [NPAIR, 128, DC, 128])
    wk_h = dram_in("wk_h", [NPAIR, 128, DC, 128])
    wv_h = dram_in("wv_h", [DC, 128, 512])
    wr_h = dram_in("wr_h", [DC, 128, 8])
    wa_h = dram_in("wa_h", [DC, 128, 128])
    wcat_h = dram_in("wcat_h", [32, 128, 37, 128])
    bqT_h = dram_in("bqT_h", [128, NPAIR])
    bkT_h = dram_in("bkT_h", [128, NPAIR])
    bv_h = dram_in("bv_h", [1, 512])
    bbT_h = dram_in("bbT_h", [128, 32])
    wobT_h = dram_in("wobT_h", [128, 32])
    ident_h = dram_in("ident_h", [128, 128])
    e16_h = dram_in("e16_h", [8, 128], BF16)
    sel_h = dram_in("sel_h", [32, 2048], BF16)

    outT = nc.dram_tensor("outT", [D, NROW], F32, kind="ExternalOutput")

    cc_kin = nc.dram_tensor("cc_kin", [NPAIR, 128, NROW], BF16)
    cc_kout = nc.dram_tensor("cc_kout", [NPAIR, 512, NROW], BF16)
    cc_vin = nc.dram_tensor("cc_vin", [NPAIR, NROW, 128], BF16)
    cc_vout = nc.dram_tensor("cc_vout", [NPAIR, SEQ, 128], BF16)
    mxscr = nc.dram_tensor("mxscr", [H, 16, 128], BF16)
    qdram = nc.dram_tensor("qdram", [H, 64, NROW], BF16)
    den_dram = nc.dram_tensor("den_dram", [32, 512], F32)

    dbg = {}
    if debug:
        dbg["xn"] = nc.dram_tensor("d_xn", [128, DC, NROW], BF16, kind="ExternalOutput")
        dbg["hp"] = nc.dram_tensor("d_hp", [128, NROW], BF16, kind="ExternalOutput")
        dbg["ot"] = nc.dram_tensor("d_ot", [128, NPAIR, NROW], BF16, kind="ExternalOutput")

    sc = Sched(nc)
    A = sc.add

    with ExitStack() as top:
        tsb = lambda n, s, d: top.enter_context(nc.sbuf_tensor(n, s, d))
        ident = tsb("ident", [128, 128], F32)
        ones128 = tsb("ones128", [128, 1], BF16)
        ones1x128 = tsb("ones1x128", [1, 128], BF16)
        ones1x64 = tsb("ones1x64", [1, 64], BF16)
        ot = tsb("ot", [128, NPAIR, NROW], BF16)
        hp = tsb("hp", [128, NROW], BF16)

        s_ident = A("sync", lambda e: e.dma_start(out=ident[:, :], in_=ident_h[:, :]), 16, sem="ld1")
        LD1 = [s_ident]
        m_o1 = A("vector", lambda e: e.memset(ones128[:, :], 1.0), 1)
        m_o2 = A("vector", lambda e: e.memset(ones1x128[:, :], 1.0), 1)
        A("vector", lambda e: e.memset(ones1x64[:, :], 1.0), 1)

        xn = tsb("xn", [128, DC, NROW], BF16)

        with ExitStack() as ph:
            psb = lambda n, s, d: ph.enter_context(nc.sbuf_tensor(n, s, d))
            pps = lambda n, s, d: ph.enter_context(nc.psum_tensor(n, s, d))
            xst1 = psb("xst1", [128, 2, 4, 256], F32)
            xst2 = psb("xst2", [128, 2, 4, 256], F32)
            xbb = psb("xbb", [128, 2, 4, 256], BF16)
            sqb = psb("sqb", [128, 2, 4, 256], BF16)
            mr = psb("mr", [1, 512], F32)
            mrb = psb("mrb", [1, 512], BF16)
            ps_stat = [pps("ps_stat0", [1, 512], F32), pps("ps_stat1", [1, 512], F32)]
            ps_bc = [pps("ps_bc0", [128, 512], F32), pps("ps_bc1", [128, 512], F32)]

            NRT, NCG = 8, 8
            use1 = {}
            use2 = {}
            stat_free = {}
            bcast_free = {}
            bmm_of = {}

            def stats_part(rt):
                rsl = slice(rt * 256, (rt + 1) * 256)
                spar = rt % 2
                last_mm = None
                for cg in range(NCG):
                    par = cg % 2
                    d1 = A("sync", lambda e, cg=cg, rsl=rsl, par=par: e.dma_start(
                        out=xst1[:, par], in_=xt.ap()[cg * 512:(cg + 1) * 512, rsl]
                        .rearrange("(c p) r -> p c r", p=128)), 16,
                        deps=[use1.get(par)], sem=f"xs1{par}")
                    cst = A("gpsimd", lambda e, par=par: e.tensor_copy(
                        xbb[:, par], xst1[:, par]), 1, deps=[d1, use1.get(par)])
                    sq = A("scalar", lambda e, par=par: e.activation(
                        sqb[:, par], xst1[:, par], AF.Square), 1,
                        deps=[d1, use1.get(par)])
                    for j in range(4):
                        first = (cg == 0 and j == 0)
                        last = (cg == NCG - 1 and j == 3)
                        A("tensor", lambda e, par=par, j=j, spar=spar, first=first:
                          e.matmul(ps_stat[spar][:, 0:256], ones128[:, :],
                                   xbb[:, par, j], start=first, stop=False),
                          1, deps=[cst, m_o1] +
                          ([stat_free.get(spar)] if first else []))
                        last_mm = A(
                            "tensor", lambda e, par=par, j=j, spar=spar, last=last:
                            e.matmul(ps_stat[spar][:, 256:512], ones128[:, :],
                                     sqb[:, par, j], start=False, stop=last),
                            1, deps=[sq])
                    use1[par] = last_mm
                v1 = A("vector", lambda e, spar=spar: e.tensor_scalar(
                    mr[:, 0:256], ps_stat[spar][0:1, 0:256], 1.0 / 4096.0,
                    None, ALU.mult), 1, deps=[last_mm])
                v2 = A("vector", lambda e: e.tensor_mul(
                    mr[:, 256:512], mr[:, 0:256], mr[:, 0:256]), 1, deps=[v1])
                v3 = A("vector", lambda e, spar=spar: e.scalar_tensor_tensor(
                    mr[:, 256:512], ps_stat[spar][0:1, 256:512], 1.0 / 4096.0,
                    mr[:, 256:512], ALU.mult, ALU.subtract), 1, deps=[v2])
                s1 = A("scalar", lambda e: e.activation(
                    mr[:, 256:512], mr[:, 256:512], AF.Sqrt, bias=f32i(1e-5)),
                    1, deps=[v3])
                v4 = A("vector", lambda e: e.reciprocal(
                    mr[:, 256:512], mr[:, 256:512]), 1, deps=[s1])
                v5 = A("vector", lambda e: e.tensor_copy(mrb[:, :], mr[:, :]), 1,
                       deps=[v4])
                stat_free[spar] = v5
                bmm_of[rt] = A("tensor", lambda e, spar=spar: e.matmul(
                    ps_bc[spar][:, :], ones1x128[:, :], mrb[:, :],
                    start=True, stop=True), 1,
                    deps=[v5, m_o2, bcast_free.get(spar)])

            def apply_part(rt):
                rsl = slice(rt * 256, (rt + 1) * 256)
                spar = rt % 2
                bmm = bmm_of[rt]
                lastt = None
                for cg in range(NCG):
                    par = cg % 2
                    d2 = A("scalar", lambda e, cg=cg, rsl=rsl, par=par: e.dma_start(
                        out=xst2[:, par], in_=xt.ap()[cg * 512:(cg + 1) * 512, rsl]
                        .rearrange("(c p) r -> p c r", p=128)), 16,
                        deps=[use2.get(par)], sem=f"xs2{par}")
                    for j in range(4):
                        c = cg * 4 + j
                        t1 = A("vector", lambda e, par=par, j=j, spar=spar:
                               e.tensor_sub(xst2[:, par, j], xst2[:, par, j],
                                            ps_bc[spar][:, 0:256]),
                               1, deps=[d2, bmm])
                        lastt = A("vector", lambda e, par=par, j=j, c=c, rsl=rsl,
                                  spar=spar: e.tensor_mul(
                                      xn[:, c, rsl], xst2[:, par, j],
                                      ps_bc[spar][:, 256:512]), 1, deps=[t1])
                    use2[par] = lastt
                bcast_free[spar] = lastt

            for rt in range(NRT):
                stats_part(rt)
                if rt > 0:
                    apply_part(rt - 1)
            apply_part(NRT - 1)
            sc.emit_pending()
        sc.barrier()

        sc.barrier()

        k_coll = {}
        with ExitStack() as ph:
            psb = lambda n, s, d: ph.enter_context(nc.sbuf_tensor(n, s, d))
            pps = lambda n, s, d: ph.enter_context(nc.psum_tensor(n, s, d))
            wqs = psb("wqs", [128, 2, DC, 128], BF16)
            wks = psb("wks", [128, 2, DC, 128], BF16)
            kown = psb("kown", [128, NROW], BF16)
            qstage = psb("qstage", [64, 2, 512], BF16)
            bqT = psb("bqT", [128, NPAIR], F32)
            bkT = psb("bkT", [128, NPAIR], F32)
            LD1.append(A("sync", lambda e: e.dma_start(out=bqT[:, :],
                                                       in_=bqT_h[:, :]), 16,
                         sem="ld1"))
            LD1.append(A("sync", lambda e: e.dma_start(out=bkT[:, :],
                                                       in_=bkT_h[:, :]), 16,
                         sem="ld1"))
            ps_q = [pps("ps_q0", [128, 512], F32), pps("ps_q1", [128, 512], F32)]
            ps_k = [pps("ps_k0", [128, 512], F32), pps("ps_k1", [128, 512], F32)]

            wfree = {}
            qfree = {}
            kfree = {}
            qdma = {}
            kd = None
            for p in range(NPAIR):
                par = p % 2
                dq = A("gpsimd", lambda e, p=p, par=par: e.dma_start(
                    out=wqs[:, par], in_=wq_h.ap()[p]), 16,
                    deps=[wfree.get(("q", par))], sem=f"wq{par}")
                dk = A("gpsimd", lambda e, p=p, par=par: e.dma_start(
                    out=wks[:, par], in_=wk_h.ap()[p]), 16,
                    deps=[wfree.get(("k", par))], sem=f"wk{par}")
                qmm = kmm = None
                for rt in range(4):
                    rsl = slice(rt * 512, (rt + 1) * 512)
                    rp = rt % 2
                    for c in range(DC):
                        qmm = A("tensor", lambda e, par=par, c=c, rsl=rsl, rp=rp:
                                e.matmul(ps_q[rp][:, :], wqs[:, par, c],
                                         xn[:, c, rsl], start=(c == 0),
                                         stop=(c == DC - 1)), 1,
                                deps=([dq, qfree.get(rp)] if c == 0 else []))
                    for c in range(DC):
                        kmm = A("tensor", lambda e, par=par, c=c, rsl=rsl, rp=rp:
                                e.matmul(ps_k[rp][:, :], wks[:, par, c],
                                         xn[:, c, rsl], start=(c == 0),
                                         stop=(c == DC - 1)), 1,
                                deps=([dk, kfree.get(rp)] if c == 0 else []))
                    eva = A("vector", lambda e, p=p, rp=rp: e.tensor_scalar(
                        qstage[:, 0, :], ps_q[rp][0:64, :],
                        bqT[0:64, p:p + 1], None, ALU.add), 1,
                        deps=[qmm] + LD1 + [qdma.get(0)])
                    evb = A("vector", lambda e, p=p, rp=rp: e.tensor_scalar(
                        qstage[:, 1, :], ps_q[rp][64:128, :],
                        bqT[64:128, p:p + 1], None, ALU.add), 1,
                        deps=[qmm, qdma.get(1)])
                    qdma[0] = A("sync", lambda e, p=p, rsl=rsl: e.dma_start(
                        out=qdram.ap()[2 * p][:, rsl], in_=qstage[:, 0, :]), 16,
                        deps=[eva], sem="qo0")
                    qdma[1] = A("sync", lambda e, p=p, rsl=rsl: e.dma_start(
                        out=qdram.ap()[2 * p + 1][:, rsl], in_=qstage[:, 1, :]), 16,
                        deps=[evb], sem="qo1")
                    ev2 = evb
                    ev3 = A("vector", lambda e, p=p, rsl=rsl, rp=rp: e.tensor_scalar(
                        kown[:, rsl], ps_k[rp][:, :],
                        bkT[:, p:p + 1], None, ALU.add), 1, deps=[kmm] + LD1 + [kd])
                    qfree[rp] = ev2
                    kfree[rp] = ev3
                wfree[("q", par)] = qmm
                wfree[("k", par)] = kmm
                kd = A("sync", lambda e, p=p: e.dma_start(
                    out=cc_kin.ap()[p], in_=kown[:, :]), 16,
                    deps=[kfree[0], kfree[1]], sem="kout")
                k_coll[p] = A("gpsimd", lambda e, p=p: e.collective_compute(
                    "AllGather", ALU.bypass,
                    replica_groups=[[0, 1, 2, 3], [4, 5, 6, 7]],
                    ins=[cc_kin.ap()[p]], outs=[cc_kout.ap()[p]]), 1,
                    deps=[kd], sem=f"kc{p}")
            sc.emit_pending()
        sc.barrier()

        v_coll = {}
        with ExitStack() as ph:
            psb = lambda n, s, d: ph.enter_context(nc.sbuf_tensor(n, s, d))
            pps = lambda n, s, d: ph.enter_context(nc.psum_tensor(n, s, d))
            wvs = psb("wvs", [128, DC, 512], BF16)
            wrs = psb("wrs", [128, DC, 8], BF16)
            was = psb("was", [128, DC, 128], BF16)
            e16 = psb("e16", [8, 128], BF16)
            bvb = psb("bvb", [128, 512], BF16)
            LD1.append(A("sync", lambda e: e.dma_start(out=e16[:, :],
                                                       in_=e16_h[:, :]), 16,
                         sem="ld1"))
            vev = psb("vev", [128, 2, 512], BF16)
            rsc = psb("rsc", [128, 8], F32)
            rs2 = psb("rs2", [128, 8], F32)
            rcol = psb("rcol", [128, 1], F32)
            rcol2 = psb("rcol2", [128, 1], F32)
            combf = psb("combf", [128, 8], F32)
            combT_sb = psb("combT_sb", [8, 512], BF16)
            exq = psb("exq", [128, 4, 512], BF16)
            bvrow = psb("bvrow", [1, 512], BF16)
            ps_v = pps("ps_v", [128, 512], F32)
            ps_lg = pps("ps_lg", [128, 8], F32)
            ps_ct = pps("ps_ct", [8, 512], F32)
            ps_ex = pps("ps_ex", [128, 512], F32)
            ps_hh = pps("ps_hh", [128, 512], F32)

            B2W = []
            for i in range(4):
                B2W.append(A("gpsimd", lambda e, i=i: e.dma_start(
                    out=wvs[:, i * 8:(i + 1) * 8, :],
                    in_=wv_h.ap()[i * 8:(i + 1) * 8].rearrange("c p m -> p c m")),
                    16, sem="b2w"))
            B2W.append(A("gpsimd", lambda e: e.dma_start(
                out=wrs[:, :, :], in_=wr_h.ap().rearrange("c p m -> p c m")), 16,
                sem="b2w"))
            for i in range(4):
                B2W.append(A("gpsimd", lambda e, i=i: e.dma_start(
                    out=was[:, i * 8:(i + 1) * 8, :],
                    in_=wa_h.ap()[i * 8:(i + 1) * 8].rearrange("c p m -> p c m")),
                    16, sem="b2w"))
            B2W.append(A("gpsimd", lambda e: e.dma_start(
                out=bvrow[:, :], in_=bv_h[:, :]), 16, sem="b2w"))
            bvmm = A("tensor", lambda e: e.matmul(
                ps_ex[:, :], ones1x128[:, :], bvrow[:, :], start=True, stop=True),
                1, deps=B2W)
            bvcp = A("vector", lambda e: e.tensor_copy(bvb[:, :], ps_ex[:, :]),
                     1, deps=[bvmm])

            ev_prev = None
            ct_free = None
            exq_ids = []
            vd = None
            lg_free = []
            for rt5 in range(4):
                ct_ids = []
                for sub in range(4):
                    rt = rt5 * 4 + sub
                    rsl = slice(rt * 128, (rt + 1) * 128)
                    vmm = lgmm = None
                    for c in range(DC):
                        vmm = A("tensor", lambda e, c=c, rsl=rsl: e.matmul(
                            ps_v[:, :], xn[:, c, rsl], wvs[:, c, :],
                            start=(c == 0), stop=(c == DC - 1)), 1,
                            deps=(B2W + [ev_prev] if c == 0 else []))
                    for c in range(DC):
                        lgmm = A("tensor", lambda e, c=c, rsl=rsl: e.matmul(
                            ps_lg[:, :], xn[:, c, rsl], wrs[:, c, :],
                            start=(c == 0), stop=(c == DC - 1)), 1,
                            deps=((B2W + lg_free) if c == 0 else []))
                    ev = A("vector", lambda e, sub=sub: e.tensor_add(
                        vev[:, sub % 2, :], ps_v[:, :], bvb[:, :]), 1,
                        deps=[vmm, bvcp, vd])
                    ev_prev = ev
                    vd = A("sync", lambda e, rt=rt, sub=sub: e.dma_start(
                        out=cc_vin.ap().rearrange("q (t p) e -> t p q e", p=128)[rt],
                        in_=vev[:, sub % 2, :]
                        .rearrange("p (q e) -> p q e", q=4)), 16, deps=[ev],
                        sem="vout")
                    r1 = A("vector", lambda e: e.reduce_max(rcol[:, :], ps_lg[:, :],
                                                            axis=AX), 1, deps=[lgmm])
                    r2 = A("vector", lambda e: e.tensor_single_scalar(
                        rsc[:, :], ps_lg[:, :], rcol[:, 0:1], ALU.is_equal),
                        1, deps=[r1])
                    r3 = A("vector", lambda e: e.scalar_tensor_tensor(
                        rs2[:, :], rsc[:, :], -1e30, ps_lg[:, :],
                        ALU.mult, ALU.add), 1, deps=[r2])
                    r4 = A("vector", lambda e: e.reduce_max(rcol2[:, :], rs2[:, :],
                                                            axis=AX), 1, deps=[r3])
                    r5 = A("vector", lambda e: e.tensor_single_scalar(
                        rsc[:, :], ps_lg[:, :], rcol2[:, 0:1], ALU.is_ge),
                        1, deps=[r4])
                    r6 = A("vector", lambda e: e.tensor_scalar(
                        rcol[:, :], rcol[:, :], -1.0, None, ALU.mult),
                        1, deps=[r2])
                    s2 = A("scalar", lambda e: e.activation(
                        rs2[:, :], ps_lg[:, :], AF.Exp, bias=rcol[:, 0:1]),
                        1, deps=[r6, r4])
                    s3 = A("scalar", lambda e: e.activation(
                        rcol2[:, :], rcol2[:, :], AF.Exp, bias=rcol[:, 0:1]),
                        1, deps=[s2, r5])
                    r7 = A("vector", lambda e: e.tensor_scalar(
                        rcol2[:, :], rcol2[:, :], 1.0, None, ALU.add),
                        1, deps=[s3])
                    r8 = A("vector", lambda e: e.reciprocal(rcol2[:, :],
                                                            rcol2[:, :]),
                           1, deps=[r7])
                    r9 = A("vector", lambda e: e.tensor_mul(
                        rs2[:, :], rs2[:, :], rsc[:, :]), 1, deps=[s2, r5])
                    r10 = A("vector", lambda e: e.tensor_scalar(
                        combf[:, :], rs2[:, :], rcol2[:, 0:1], None, ALU.mult),
                        1, deps=[r9, r8])
                    lg_free = [s2, r5]
                    tmm = A("tensor", lambda e, sub=sub: e.transpose(
                        ps_ct[:, sub * 128:(sub + 1) * 128], combf[:, :],
                        ident[:, :]), 1, deps=[r10, ct_free] + LD1)
                    ct_ids.append(tmm)
                cte = A("vector", lambda e: e.tensor_copy(combT_sb[:, :],
                                                          ps_ct[:, :]),
                        1, deps=ct_ids)
                ct_free = cte
                exm = A("tensor", lambda e: e.matmul(
                    ps_ex[:, :], e16[:, :], combT_sb[:, :], start=True, stop=True),
                    1, deps=[cte, bvcp] + LD1)
                exq_ids.append(A("vector", lambda e, rt5=rt5: e.tensor_copy(
                    exq[:, rt5, :], ps_ex[:, :]), 1, deps=[exm]))
            hh_free = None
            hp_done = None
            for rt5 in range(4):
                hmm = None
                for c in range(DC):
                    hmm = A("tensor", lambda e, c=c, rt5=rt5: e.matmul(
                        ps_hh[:, :], was[:, c, :],
                        xn[:, c, rt5 * 512:(rt5 + 1) * 512],
                        start=(c == 0), stop=(c == DC - 1)), 1,
                        deps=(B2W + [exq_ids[rt5], hh_free]
                              if c == 0 else []))
                hp_done = A("vector", lambda e, rt5=rt5: e.tensor_mul(
                    hp[:, rt5 * 512:(rt5 + 1) * 512], ps_hh[:, :],
                    exq[:, rt5, :]), 1, deps=[hmm])
                hh_free = hp_done
            for p in range(NPAIR):
                v_coll[p] = A("gpsimd", lambda e, p=p: e.collective_compute(
                    "AllGather", ALU.bypass,
                    replica_groups=[[0, 1, 2, 3], [4, 5, 6, 7]],
                    ins=[cc_vin.ap()[p]], outs=[cc_vout.ap()[p]]), 1,
                    deps=[vd], sem=f"vc{p}")
            if debug:
                A("sync", lambda e: e.dma_start(out=dbg["xn"].ap(), in_=xn.ap()),
                  16, deps=[hp_done], sem="dbgd")
                A("sync", lambda e: e.dma_start(out=dbg["hp"].ap(), in_=hp.ap()),
                  16, deps=[hp_done], sem="dbgd")
            sc.emit_pending()
        sc.barrier()

        with ExitStack() as ph:
            psb = lambda n, s, d: ph.enter_context(nc.sbuf_tensor(n, s, d))
            pps = lambda n, s, d: ph.enter_context(nc.psum_tensor(n, s, d))
            ktx = psb("ktx", [128, 2, SEQ], BF16)
            qtxh = psb("qtxh", [128, 2, NROW], BF16)
            vst = psb("vst", [128, 64, 65], BF16)
            pt = psb("pt", [128, 1536], BF16)
            MX16 = psb("MX16", [128, 11], F32)
            MX = psb("MX", [128, 16], F32)
            MXT = psb("MXT", [16, 128], BF16)
            denst = psb("denst", [1, 512], F32)
            ps_nat = [pps("ps_natA", [128, 1024], F32),
                      pps("ps_natB", [128, 512], F32)]
            ps_s = [pps("ps_sA", [128, 1024], F32),
                    pps("ps_sB", [128, 512], F32)]
            ps_o = [pps("ps_oA", [65, 512], F32), pps("ps_oB", [65, 512], F32)]

            mszk = A("vector", lambda e: e.memset(ktx[64:128, :, :], 0.0), 1)
            mszq = A("vector", lambda e: e.memset(qtxh[64:128, :, :], 0.0), 1)
            mset0 = A("vector", lambda e: e.memset(ktx[64:65, 0, :], 1.0), 1,
                      deps=[mszk])
            mset1 = A("vector", lambda e: e.memset(ktx[64:65, 1, :], 1.0), 1,
                      deps=[mszk])
            msetv = A("vector", lambda e: e.memset(vst[:, :, 64:65], 1.0), 1)

            st = {
                "ktx_free": {0: mset0, 1: mset1},
                "q_free": {0: mszq, 1: mszq},
                "vst_free": msetv,
                "pt_free": {0: None, 1: None},
                "nat_free": {0: None, 1: None},
                "o_free": {0: None, 1: None},
                "mx_free": None,
                "mx16_free": None,
                "exp_last": None,
                "s_free": {0: None, 1: None},
                "sbuf_rr": 0,
                "dend": None,
                "kld": {}, "vld": {}, "qtxrow": {},
            }

            def load_ktx(h):
                p, eo, b = h // 2, (h % 2) * 64, h % 2
                kdm = A("sync", lambda e, p=p, eo=eo, b=b: e.dma_start(
                    out=ktx[0:64, b, :].rearrange("p (r f) -> p r f", r=4),
                    in_=cc_kout.ap()[p].rearrange("(r p) f -> r p f", p=128)
                    [:, eo:eo + 64, :].rearrange("r p f -> p r f")), 16,
                    deps=[k_coll[p], st["ktx_free"][b]], sem=f"kt{b}")
                qd = A("sync", lambda e, h=h, b=b: e.dma_start(
                    out=qtxh[0:64, b, :], in_=qdram.ap()[h]), 16,
                    deps=[st["q_free"][b], qdma[0], qdma[1]], sem=f"qt{b}")
                qz = A("vector", lambda e, b=b: e.memset(
                    qtxh[64:65, b, :], 0.0), 1, deps=[st["q_free"][b]])
                st["kld"][h] = [kdm, qd, qz]

            def load_vst(h):
                p, eo = h // 2, (h % 2) * 64
                last = None
                for q in range(4):
                    last = A("sync", lambda e, p=p, eo=eo, q=q: e.dma_start(
                        out=vst[:, q * 16:(q + 1) * 16, 0:64],
                        in_=cc_vout.ap()[p][q * 2048:(q + 1) * 2048, eo:eo + 64]
                        .rearrange("(c p) e -> p c e", p=128)), 16,
                        deps=[v_coll[p], st["vst_free"]], sem="vs0")
                st["vld"][h] = last

            def nat_grp(h, qt, gi):
                b = h % 2
                buf = gi % 2
                if buf == 0:
                    nch = 2 if gi < 10 else 1
                    base = (gi // 2) * 3
                else:
                    nch = 1
                    base = (gi // 2) * 3 + 2
                mm = None
                for j in range(nch):
                    scn = base + j
                    mm = A("tensor", lambda e, h=h, qt=qt, scn=scn,
                           b=b, buf=buf, j=j: e.matmul(
                               ps_nat[buf][:, j * 512:(j + 1) * 512],
                               qtxh[:, b, qt * 128:(qt + 1) * 128],
                               ktx[:, b, scn * 512:(scn + 1) * 512],
                               start=True, stop=True),
                           (1 if j == nch - 1 else 0),
                           deps=st["kld"][h] +
                           ([st["nat_free"][buf]] if j == 0 else []))
                r = A("vector", lambda e, gi=gi, buf=buf, nch=nch: e.reduce_max(
                    MX16[:, gi:gi + 1], ps_nat[buf][:, 0:nch * 512], axis=AX),
                    1, deps=[mm] + ([st["mx16_free"]] if gi == 0 else []))
                st["nat_free"][buf] = r
                return r

            def nat_fin(h, qt, lastr):
                fin = A("vector", lambda e, qt=qt: e.reduce_max(
                    MX[:, qt:qt + 1], MX16[:, 0:11], axis=AX), 1,
                    deps=[lastr, st["mx_free"]])
                st["mx16_free"] = fin
                return fin

            def mx_finalize(h, lastred):
                tr = A("tensor", lambda e: e.transpose(
                    ps_nat[1][0:16, 0:128], MX[:, :], ident[:, :]), 1,
                    deps=[lastred, st["nat_free"][1]] + LD1)
                ng = A("scalar", lambda e: e.mul(MXT[:, :],
                                                 ps_nat[1][0:16, 0:128],
                                                 f32i(-1.0)), 1, deps=[tr])
                st["nat_free"][1] = ng
                st["mx_free"] = ng
                d1 = A("gpsimd", lambda e, h=h: e.dma_start(
                    out=mxscr.ap()[h], in_=MXT[:, :]), 16, deps=[ng], sem="mxs")
                st["qtxrow"][h] = A("gpsimd", lambda e, h=h, b=h % 2: e.dma_start(
                    out=qtxh[64:65, b, :],
                    in_=mxscr.ap()[h].rearrange("a b -> (a b)")
                    .rearrange("(o f) -> o f", o=1)), 16, deps=[d1],
                    sem=f"qr{h % 2}")

            PTOFF = {0: 0, 1: 1024}
            SBW = {0: 2, 1: 1}

            def emit_attnv(ts, sub, grp):
                h = ts["h"]
                b = h % 2
                buf, nch, coff, ex = grp
                ob = sub["ob"]
                poff = PTOFF[buf]
                for j in range(nch):
                    scn = coff + j
                    first = (coff == 0 and j == 0)
                    last = (coff + j == 63)
                    av = A("tensor", lambda e, b=b, scn=scn, j=j, poff=poff,
                           ob=ob, first=first, last=last: e.matmul(
                               ps_o[ob][:, :], vst[:, scn, 0:65],
                               pt[:, poff + j * 512:poff + (j + 1) * 512],
                               start=first, stop=last),
                           (1 if j == nch - 1 else 0),
                           deps=([ex, st["vld"][h]] +
                                 ([st["o_free"][ob]] if first else [])
                                 if j == 0 else []))
                    st["pt_free"][buf] = av
                    sub["av_last"] = av

            def drain_buf(ts, buf):
                for su in ts["subs"]:
                    if su["grpq"] and su["grpq"][0][0] == buf:
                        emit_attnv(ts, su, su["grpq"].pop(0))

            def sub_step(ts, sub):
                h = ts["h"]
                b = h % 2
                qt = sub["qt"]
                qsl = slice(qt * 512, (qt + 1) * 512)
                if sub["coff"] < 64:
                    buf = st["sbuf_rr"] % 2
                    st["sbuf_rr"] += 1
                    drain_buf(ts, buf)
                    nch = min(SBW[buf], 64 - sub["coff"])
                    coff = sub["coff"]
                    sub["coff"] += nch
                    poff = PTOFF[buf]
                    smm = None
                    for j in range(nch):
                        scn = coff + j
                        smm = A("tensor", lambda e, h=h, b=b, scn=scn, j=j,
                                buf=buf, qsl=qsl: e.matmul(
                                    ps_s[buf][:, j * 512:(j + 1) * 512],
                                    ktx[:, b, scn * 128:(scn + 1) * 128],
                                    qtxh[:, b, qsl], start=True, stop=True),
                                (1 if j == nch - 1 else 0),
                                deps=([st["s_free"][buf], st["qtxrow"][h]]
                                      + st["kld"][h] if j == 0 else []))
                    ex = A("scalar", lambda e, buf=buf, poff=poff, nch=nch:
                           e.activation(pt[:, poff:poff + nch * 512],
                                        ps_s[buf][:, 0:nch * 512], AF.Exp), 1,
                           deps=[smm, st["pt_free"][buf]])
                    st["exp_last"] = ex
                    st["s_free"][buf] = ex
                    sub["grpq"].append((buf, nch, coff, ex))
                elif sub["grpq"]:
                    emit_attnv(ts, sub, sub["grpq"].pop(0))
                if sub["coff"] >= 64 and not sub["grpq"]:
                    flush_sub(ts, sub)
                    if ts["next_qt"] < 4:
                        sub["qt"] = ts["next_qt"]
                        ts["next_qt"] += 1
                        sub["coff"] = 0
                    else:
                        sub["done"] = True

            def flush_sub(ts, sub):
                h = ts["h"]
                p, eo = h // 2, (h % 2) * 64
                qt = sub["qt"]
                ob = sub["ob"]
                qsl = slice(qt * 512, (qt + 1) * 512)
                dc = A("vector", lambda e, ob=ob: e.tensor_copy(
                    denst[:, :], ps_o[ob][64:65, :]), 1,
                    deps=[sub["av_last"], st["dend"]])
                st["dend"] = A("gpsimd", lambda e, h=h, qt=qt: e.dma_start(
                    out=den_dram.ap()[h * 4 + qt:h * 4 + qt + 1, :],
                    in_=denst[:, :]), 16, deps=[dc], sem="dnd")
                o1 = A("vector", lambda e, eo=eo, p=p, qsl=qsl, ob=ob:
                       e.tensor_copy(ot[eo:eo + 64, p, qsl],
                                     ps_o[ob][0:64, :]), 1, deps=[dc])
                st["o_free"][ob] = o1

            def tr_step(ts):
                subs = [su for su in ts["subs"] if not su["done"]]
                if not subs:
                    return False
                su = subs[ts["rr"] % len(subs)]
                ts["rr"] += 1
                sub_step(ts, su)
                return True

            def tr_advance(ts, n):
                for _ in range(n):
                    if not tr_step(ts):
                        return

            load_ktx(0)
            load_vst(0)
            st["exp_of"] = {0: None, 1: None}
            for slot in range(H + 1):
                h_nat = slot if slot < H else None
                h_tr = slot - 1 if slot >= 1 else None
                ts = ({"h": h_tr, "rr": 0, "next_qt": 2,
                       "subs": [{"qt": 0, "ob": 0, "coff": 0, "grpq": [],
                                 "av_last": None, "done": False},
                                {"qt": 1, "ob": 1, "coff": 0, "grpq": [],
                                 "av_last": None, "done": False}]}
                      if h_tr is not None else None)
                if h_nat is not None:
                    for qt in range(16):
                        lastr = None
                        for gi in range(11):
                            lastr = nat_grp(h_nat, qt, gi)
                            if ts is not None:
                                tr_advance(ts, 1)
                        lastred = nat_fin(h_nat, qt, lastr)
                    if ts is not None:
                        while tr_step(ts):
                            pass
                    mx_finalize(h_nat, lastred)
                else:
                    while tr_step(ts):
                        pass
                if h_tr is not None:
                    st["ktx_free"][h_tr % 2] = st["exp_last"]
                    st["q_free"][h_tr % 2] = st["exp_last"]
                    st["vst_free"] = max(su["av_last"] for su in ts["subs"])
                    if h_tr + 1 < H:
                        load_vst(h_tr + 1)
                if h_nat is not None and h_nat + 1 < H:
                    load_ktx(h_nat + 1)
            if debug:
                A("sync", lambda e: e.dma_start(out=dbg["ot"].ap(), in_=ot.ap()),
                  16, deps=[st["o_free"]], sem="dbgd")
            sc.emit_pending()
        sc.barrier()

        with ExitStack() as ph:
            psb = lambda n, s, d: ph.enter_context(nc.sbuf_tensor(n, s, d))
            pps = lambda n, s, d: ph.enter_context(nc.psum_tensor(n, s, d))
            denall = psb("denall", [32, 512], F32)
            recipall = psb("recipall", [32, 512], BF16)
            selc = psb("selc", [32, 2048], BF16)
            bcn = psb("bcn", [128, 2, 512], BF16)
            ps_bcn = [pps("ps_bcn0", [64, 512], F32), pps("ps_bcn1", [64, 512], F32)]

            dsel = A("sync", lambda e: e.dma_start(out=selc[:, :], in_=sel_h[:, :]),
                     16, sem="ld1")
            dden = A("sync", lambda e: e.dma_start(out=denall[:, :],
                                                   in_=den_dram.ap()), 16,
                     sem="ld1")
            nl = A("scalar", lambda e: e.activation(
                denall[:, :], denall[:, :], AF.Ln), 1, deps=[dden])
            ne = A("scalar", lambda e: e.activation(
                recipall[:, :], denall[:, :], AF.Exp, scale=f32i(-1.0),
                bias=f32i(np.log(RECIP_SQRT512))), 1, deps=[nl])
            bmf = {}
            bef = {}
            otf = {}
            for hq in range(32):
                h, qt = hq // 4, hq % 4
                eo, p = (h % 2) * 64, h // 2
                par = hq % 2
                qsl = slice(qt * 512, (qt + 1) * 512)
                bm = A("tensor", lambda e, hq=hq, par=par: e.matmul(
                    ps_bcn[par][:, :], selc[:, hq * 64:(hq + 1) * 64],
                    recipall[:, :], start=True, stop=True), 1,
                    deps=[ne, dsel, dden, bef.get(par)])
                be = A("vector", lambda e, par=par, eo=eo: e.tensor_copy(
                    bcn[eo:eo + 64, par, :], ps_bcn[par][:, :]), 1,
                    deps=[bm, otf.get(par)])
                bef[par] = be
                otf[par] = A("vector", lambda e, eo=eo, p=p, qsl=qsl, par=par:
                             e.tensor_mul(ot[eo:eo + 64, p, qsl],
                                          ot[eo:eo + 64, p, qsl],
                                          bcn[eo:eo + 64, par, :]), 1,
                             deps=[be])
            sc.emit_pending()
        sc.barrier()

        with ExitStack() as ph:
            psb = lambda n, s, d: ph.enter_context(nc.sbuf_tensor(n, s, d))
            pps = lambda n, s, d: ph.enter_context(nc.psum_tensor(n, s, d))
            wcs = psb("wcs", [128, 2, 37, 128], BF16)
            xres = psb("xres", [128, 2, 1024], F32)
            ostg = psb("ostg", [128, 4, 512], F32)
            cbias = psb("cbias", [128, 32], F32)
            cb2 = psb("cb2", [128, 32], F32)
            ps_f = [pps("ps_f0", [128, 512], F32), pps("ps_f1", [128, 512], F32)]

            xnl = []
            b1 = A("sync", lambda e: e.dma_start(out=cbias[:, :], in_=bbT_h[:, :]),
                   16, sem="xnr")
            b2 = A("sync", lambda e: e.dma_start(out=cb2[:, :], in_=wobT_h[:, :]),
                   16, sem="xnr")
            b3 = A("vector", lambda e: e.tensor_add(cbias[:, :], cbias[:, :],
                                                    cb2[:, :]), 1, deps=[b1, b2])
            wfree = {}
            xfree = {0: None, 1: None}
            evfree = {0: None, 1: None}
            od = None
            for blk in range(32):
                par = blk % 2
                wd = A("gpsimd", lambda e, blk=blk, par=par: e.dma_start(
                    out=wcs[:, par], in_=wcat_h.ap()[blk]), 16,
                    deps=[wfree.get(par)], sem=f"wc{par}")
                xd = {}
                for hf in range(2):
                    xd[hf] = A("scalar", lambda e, blk=blk, hf=hf: e.dma_start(
                        out=xres[:, hf],
                        in_=xt.ap()[blk * 128:(blk + 1) * 128,
                                    hf * 1024:(hf + 1) * 1024]), 16,
                        deps=[xfree[hf]], sem=f"xr{hf}")
                mm = None
                for rt in range(4):
                    rp = rt % 2
                    rsl = slice(rt * 512, (rt + 1) * 512)
                    for c in range(DC):
                        mm = A("tensor", lambda e, par=par, c=c, rsl=rsl, rp=rp:
                               e.matmul(ps_f[rp][:, :], wcs[:, par, c],
                                        xn[:, c, rsl], start=(c == 0),
                                        stop=False), 1,
                               deps=([wd, evfree[rp], b1, b2] + xnl
                                     if c == 0 else []))
                    mm = A("tensor", lambda e, par=par, rsl=rsl, rp=rp:
                           e.matmul(ps_f[rp][:, :], wcs[:, par, 32],
                                    hp[:, rsl], start=False, stop=False), 1)
                    for j in range(4):
                        mm = A("tensor", lambda e, par=par, j=j, rsl=rsl, rp=rp:
                               e.matmul(ps_f[rp][:, :], wcs[:, par, 33 + j],
                                        ot[:, j, rsl], start=False,
                                        stop=(j == 3)), 1)
                    ev = A("vector", lambda e, blk=blk, rt=rt, rp=rp:
                           e.scalar_tensor_tensor(
                               ostg[:, rt, :], ps_f[rp][:, :],
                               cbias[:, blk:blk + 1],
                               xres[:, rt // 2, (rt % 2) * 512:
                                    (rt % 2) * 512 + 512],
                               ALU.add, ALU.add), 1,
                           deps=[mm, b3, xd[rt // 2], od])
                    evfree[rp] = ev
                    if rt == 1:
                        xfree[0] = ev
                    if rt == 3:
                        xfree[1] = ev
                        od = A("sync", lambda e, blk=blk: e.dma_start(
                            out=outT.ap()[blk * 128:(blk + 1) * 128, :],
                            in_=ostg.ap().rearrange("p a b -> p (a b)")), 16,
                            deps=[ev], sem="oout")
                wfree[par] = mm
            sc.emit_pending()

        sc.final_join()
    sc.close()
    return nc



def _prep_weights(ins):
    w = {}
    Wqk = np.asarray(ins["L0_attn_Wqk_w"], dtype=np.float32)
    bqk = np.asarray(ins["L0_attn_Wqk_b"], dtype=np.float32)
    Wv = np.asarray(ins["L0_attn_Wv_w"], dtype=np.float32)
    bv = np.asarray(ins["L0_attn_Wv_b"], dtype=np.float32)
    Wo = np.asarray(ins["L0_attn_Wo_w"], dtype=np.float32)
    bo = np.asarray(ins["L0_attn_Wo_b"], dtype=np.float32)
    Wb = np.asarray(ins["L0_ffn_W_base"], dtype=np.float32)
    bb = np.asarray(ins["L0_ffn_b_base"], dtype=np.float32)
    Wr = np.asarray(ins["L0_ffn_W_router"], dtype=np.float32)
    A_ = np.asarray(ins["L0_ffn_A_stack"], dtype=np.float32)
    B_ = np.asarray(ins["L0_ffn_B_stack"], dtype=np.float32)

    Wq = Wqk.reshape(H, E, 2, D)[:, :, 0, :].reshape(512, D)
    Wk = Wqk.reshape(H, E, 2, D)[:, :, 1, :].reshape(512, D)
    bq = bqk.reshape(H, E, 2)[:, :, 0].reshape(512)
    bk = bqk.reshape(H, E, 2)[:, :, 1].reshape(512)

    w["wq_h"] = np.ascontiguousarray(
        Wq.reshape(NPAIR, 128, DC, 128).transpose(0, 3, 2, 1))
    w["wk_h"] = np.ascontiguousarray(
        Wk.reshape(NPAIR, 128, DC, 128).transpose(0, 3, 2, 1))
    w["wv_h"] = np.ascontiguousarray(Wv.T.reshape(DC, 128, 512))
    w["wr_h"] = np.ascontiguousarray(Wr.T.reshape(DC, 128, 8))
    A_all = A_.reshape(128, D)
    w["wa_h"] = np.ascontiguousarray(A_all.T.reshape(DC, 128, 128))
    B_allT = B_.transpose(0, 2, 1).reshape(128, D)
    Kcat = np.concatenate([Wb.T, B_allT, Wo.T], axis=0)
    w["wcat_h"] = np.ascontiguousarray(
        Kcat.reshape(37, 128, 32, 128).transpose(2, 1, 0, 3))
    w["bqT_h"] = np.ascontiguousarray(bq.reshape(NPAIR, 128).T)
    w["bkT_h"] = np.ascontiguousarray(bk.reshape(NPAIR, 128).T)
    w["bv_h"] = np.ascontiguousarray(bv.reshape(1, 512))
    w["bbT_h"] = np.ascontiguousarray(bb.reshape(32, 128).T)
    w["wobT_h"] = np.ascontiguousarray(bo.reshape(32, 128).T)
    w["ident_h"] = np.eye(128, dtype=np.float32)
    sel = np.zeros((32, 32 * 64), dtype=np.float32)
    for k in range(32):
        sel[k, k * 64:(k + 1) * 64] = 1.0
    w["sel_h"] = sel.astype(ml_dtypes.bfloat16)
    e16 = np.zeros((8, 128), dtype=np.float32)
    for ex in range(8):
        e16[ex, ex * 16:(ex + 1) * 16] = 2.0
    w["e16_h"] = e16.astype(ml_dtypes.bfloat16)
    return w


def kernel(**inputs):
    x = np.asarray(inputs["L0_attn_x"], dtype=np.float32).reshape(NTOK, D)
    w = _prep_weights(inputs)
    nc = build_program(debug=False)
    in_maps = []
    for c in range(NCORES):
        m = dict(w)
        m["xt"] = np.ascontiguousarray(x[c * NROW:(c + 1) * NROW, :].T)
        in_maps.append(m)
    res = run_bass_kernel_spmd(nc, in_maps, core_ids=list(range(NCORES)))
    out = np.empty((NTOK, D), dtype=np.float32)
    for c in range(NCORES):
        out[c * NROW:(c + 1) * NROW, :] = res.results[c]["outT"].T
    return out.reshape(2, 8192, D)
